# version 01
# baseline (speedup 1.0000x reference)
import numpy as np
import ml_dtypes
import concourse.bass as bass
import concourse.mybir as mybir
from concourse.bass_utils import run_bass_kernel_spmd

F32 = mybir.dt.float32
BF16 = mybir.dt.bfloat16
U8 = mybir.dt.uint8
AF = mybir.ActivationFunctionType
ALU = mybir.AluOpType
AXX = mybir.AxisListType.X

D = 2048
NH = 16
HID = 8192
TW = 512
TS = 1024
TT = TW + TS
C = 64
NCH = TT // C
WCH = TW // C
NKB = D // 128
NTIL = TT // 128
OTIL = TS // 128
DKS = 0.08838834764831845


class Plan:
    def __init__(self):
        self.items = {"sync": [], "gpsimd": [], "tensor": [], "vector": [], "scalar": []}

    def emit(self, eng, fn):
        self.items[eng].append(fn)


class Sem:
    def __init__(self, handle):
        self.h = handle
        self.n = 0


def build(dbg=False, stop_phase=99):
    nc = bass.Bass("TRN2", target_bir_lowering=False)

    x_d = nc.dram_tensor("x", [TT, D], F32, kind="ExternalInput").ap()
    wq_d = nc.dram_tensor("wq", [D, D], F32, kind="ExternalInput").ap()
    wf_d = nc.dram_tensor("wf", [D, D], F32, kind="ExternalInput").ap()
    wi_d = nc.dram_tensor("wi", [D, D], F32, kind="ExternalInput").ap()
    wo_d = nc.dram_tensor("wo", [D, D], F32, kind="ExternalInput").ap()
    gw_d = nc.dram_tensor("gw", [D], F32, kind="ExternalInput").ap()
    wgt_d = nc.dram_tensor("wgt", [D, HID], F32, kind="ExternalInput").ap()
    wut_d = nc.dram_tensor("wut", [D, HID], F32, kind="ExternalInput").ap()
    wdt_d = nc.dram_tensor("wdt", [HID, D], F32, kind="ExternalInput").ap()
    wmask_d = nc.dram_tensor("wmask", [128, 1], F32, kind="ExternalInput").ap()
    out_d = nc.dram_tensor("out", [TS, D], F32, kind="ExternalOutput").ap()

    v_dram = nc.dram_tensor("v_sc", [TT, D], F32, kind="Internal").ap()
    kd_dram = nc.dram_tensor("kd_sc", [D, TT], F32, kind="Internal").ap()
    ktt_dram = nc.dram_tensor("ktt_sc", [NH, TT, 128], F32, kind="Internal").ap()
    lam_dram = nc.dram_tensor("lam_sc", [D, TS], F32, kind="Internal").ap()
    qt_dram = nc.dram_tensor("qt_sc", [D, TS], F32, kind="Internal").ap()
    nt_dram = nc.dram_tensor("nt_sc", [128, NKB * TS], BF16, kind="Internal").ap()
    ffn_dram = nc.dram_tensor("ffn_sc", [TS, D], F32, kind="Internal").ap()
    invs_dram = nc.dram_tensor("invs_sc", [128, OTIL], F32, kind="Internal").ap()
    invsr_dram = nc.dram_tensor("invsr_sc", [1, TS], F32, kind="Internal").ap()

    ident_d = nc.inline_tensor(np.eye(128, dtype=np.float32), "ident").ap()
    tri_d = nc.inline_tensor(np.triu(np.ones((C, C), dtype=np.float32)), "tri").ap()

    dbg_d = {}
    if dbg:
        for nm, shp in [("d_xn", [128, D]), ("d_n", [128, D]), ("d_e", [128, TT]),
                        ("d_kt", [128, TT]), ("d_lam", [128, TS]),
                        ("d_qt", [128, TS]), ("d_H", [128, 128]), ("d_O", [128, OTIL * D]),
                        ("d_oT", [128, OTIL * D]), ("d_ffn", [128, D]), ("d_v", [128, NTIL * 128]),
                        ("d_at", [128, TS]), ("d_ot2", [128, NKB * TS]),
                        ("d_skd", [128, TT]), ("d_skt", [64, NCH * 128]), ("d_svh", [64, NCH * 128]), ("d_sqh", [128, TS]),
                        ("d_sm", [64, C]), ("d_qh2", [128, C]), ("d_po", [128, 128]), ("d_H2", [128, 128]), ("d_Hs", [128, 6 * 128]), ("d_pd", [128, 128]), ("d_dce", [128, NCH]), ("d_kt8", [64, 128]), ("d_vh8", [64, 128])]:
            dbg_d[nm] = nc.dram_tensor(nm, shp, F32, kind="ExternalOutput").ap()

    AR_BYTES = 196608
    arena = nc.alloc_sbuf_tensor("arena", [128, AR_BYTES], U8)
    arena_ap = arena.ap()

    def vw(off, cols, dt=F32, parts=128):
        esz = 4 if dt == F32 else 2
        return arena_ap[0:parts, off:off + cols * esz].bitcast(dt)

    IDENT = vw(184320, 128)
    TRI = vw(184832, C, parts=64)
    CM1 = vw(185344, 1)
    DCL = vw(185856, NH * NCH)
    DCE = vw(187392, NH * NCH)
    DCI = vw(188928, NH * NCH)
    GWT = vw(190464, NKB)
    STA = vw(190976, 128)
    HH = [vw(191488, 128), vw(192000, 128)]
    SMB = [vw(192512, C, parts=64), vw(192768, C, parts=64)]
    QHB = [vw(193024, C), vw(193280, C)]
    WMK = vw(194048, 1)

    XT = [vw(0, D), vw(8192, D)]
    XN = vw(16384, D)
    QY = vw(24576, D)
    QB = vw(32768, D)
    QN = vw(40960, D)
    QN2 = vw(49152, D)
    NTB = vw(57344, D, BF16)
    XNT_HI = vw(61440, NKB * TT, BF16)
    XNT_LO = vw(110592, NKB * TT, BF16)
    EH = vw(159744, TT)
    GH = vw(165888, TT)
    BH = vw(172032, TT)
    LH = vw(178176, TS)
    QSH = vw(159744, TS)
    KTT = vw(0, NCH * 128, parts=64)
    WB = [vw(16384, D), vw(24576, D)]
    WSH = [vw(32768, D, BF16), vw(40960, D, BF16)]
    WSL = [vw(36864, D, BF16), vw(45056, D, BF16)]
    VSTG = vw(49152, NTIL * 128)

    SGW = vw(0, NKB * 512, BF16)
    SGU = vw(16384, NKB * 512, BF16)
    FST = [vw(32768, D)]
    UT7 = vw(40960, TS)
    INVSB = vw(45056, TS)
    SGD = [vw(49152, D, BF16), vw(53248, D, BF16),
           vw(57344, D, BF16), vw(61440, D, BF16)]
    NT7 = vw(65536, NKB * TS, BF16)
    FFA = vw(98304, OTIL * D)
    PH7 = vw(163840, 4 * TS, BF16)
    PL7 = vw(172032, 4 * TS, BF16)
    GT7 = vw(180224, TS)

    HS = []
    for s in range(2):
        base = 34816 * s
        HS.append({
            "kd": vw(base + 0, TT),
            "kt": vw(base + 6144, NCH * 128, parts=64),
            "vh": vw(base + 18432, NCH * 128, parts=64),
            "qh": vw(base + 30720, TS),
        })
    OBUF = vw(98304, OTIL * D)

    OTH = vw(0, NKB * TS, BF16)
    OTL = vw(32768, NKB * TS, BF16)
    OSQ = vw(69632, D)
    WOB = [vw(77824, D), vw(86016, D)]
    WOH = [vw(94208, D, BF16), vw(98304, D, BF16)]
    WOL = [vw(102400, D, BF16), vw(106496, D, BF16)]
    ATT = vw(114688, NKB * TS)
    XT11 = vw(0, D)
    FT11 = vw(8192, D)
    OUT11 = vw(16384, D)

    pA = nc.alloc_psum_tensor("pA", [128, 512], F32).ap()
    pB = nc.alloc_psum_tensor("pB", [128, 512], F32).ap()
    pC = nc.alloc_psum_tensor("pC", [128, 512], F32).ap()
    pD = nc.alloc_psum_tensor("pD", [128, 512], F32).ap()
    pBIG = nc.alloc_psum_tensor("pBIG", [128, 2048], F32).ap()
    PROT = [pA, pB, pC, pD]

    plan = Plan()
    S = {}

    def w(eng, sem, val=None):
        if isinstance(sem, tuple):
            sem, val = sem
        assert val is not None
        plan.emit(eng, lambda e, s=sem, v=val: e.wait_ge(s.h, v))

    def dma(eng, sem, out, in_, slow=False):
        def fn(e, out=out, in_=in_, s=sem, slow=slow):
            e.dma_start(out=out, in_=in_,
                        allow_slow_non_contiguous=slow).then_inc(s.h, 16)
        plan.emit(eng, fn)
        sem.n += 16
        return (sem, sem.n)

    def op(eng, sem, f):
        def fn(e, f=f, s=sem):
            f(e).then_inc(s.h, 1)
        plan.emit(eng, fn)
        sem.n += 1
        return (sem, sem.n)

    def opn(eng, f):
        plan.emit(eng, lambda e, f=f: f(e))

    ENG_SEM = {}

    def fence(eng):
        sem = ENG_SEM[eng]
        t = op(eng, sem, lambda e: e.engine_nop())
        w(eng, t)
        return t

    def emit_all():
        def stop_here(ph):
            return ph >= stop_phase
        sW, sG, sPE, sDVE, sACT = S["w"], S["g"], S["pe"], S["dve"], S["act"]
        ENG_SEM.update({"vector": sDVE, "scalar": sACT, "tensor": sPE, "gpsimd": sG})
        ZERO = (sDVE, 0)

        MEAN, VAR, RSTD, AMX, SSC, RMS = (STA[:, i:i + 1] for i in range(6))
        INVC = STA[:, 16:16 + OTIL]

        dma("sync", sW, IDENT, ident_d)
        dma("sync", sW, TRI, tri_d)
        dma("sync", sW, WMK, wmask_d)
        cl = dma("sync", sW, GWT, gw_d.rearrange("(k p) -> p k", p=128), slow=True)
        opn("vector", lambda e: e.memset(CM1, -1.0))
        w("vector", sW, cl[1])

        xn_done = []
        tile_pe_end = []
        tile_dve_end = []
        nt_spills = []
        last_evac = bankf = {}

        def transpose_block(src_ap, kb_src_col, px, dst_hi, dst_lo, dst_col, bankkey):
            if bankkey in last_evac:
                w("tensor", last_evac[bankkey][0], last_evac[bankkey][1])
            mm = op("tensor", sPE, lambda e: e.transpose(
                out=px[:, 0:128], in_=src_ap[:, kb_src_col:kb_src_col + 128],
                identity=IDENT))
            w("scalar", sPE, mm[1])
            hv = op("scalar", sACT, lambda e: e.copy(
                out=dst_hi[:, dst_col:dst_col + 128], in_=px[:, 0:128]))
            w("vector", sACT, hv[1])
            lv = op("vector", sDVE, lambda e: e.tensor_tensor(
                out=dst_lo[:, dst_col:dst_col + 128], in0=px[:, 0:128],
                in1=dst_hi[:, dst_col:dst_col + 128], op=ALU.subtract))
            last_evac[bankkey] = lv

        for t in range(NTIL):
            own = t >= (TW // 128)
            ot = t - TW // 128
            xt = XT[t % 2]
            if t >= 2:
                w("sync", xn_done[t - 2])
            ld = dma("sync", sW, xt, x_d[t * 128:(t + 1) * 128, :])
            if t >= 1:
                w("vector", sPE, tile_pe_end[t - 1])
            w("vector", sW, ld[1])
            opn("vector", lambda e, xt=xt: e.reduce_sum(out=MEAN, in_=xt, axis=AXX))
            fence("vector")
            opn("vector", lambda e: e.tensor_scalar(out=MEAN, in0=MEAN, scalar1=1.0 / D,
                                                    scalar2=None, op0=ALU.mult))
            fence("vector")
            opn("vector", lambda e, xt=xt: e.tensor_scalar(out=XN, in0=xt, scalar1=MEAN,
                                                           scalar2=None, op0=ALU.subtract))
            fence("vector")
            opn("vector", lambda e: e.tensor_tensor(out=QY, in0=XN, in1=XN, op=ALU.mult))
            opn("vector", lambda e: e.reduce_sum(out=VAR, in_=QY, axis=AXX))
            fence("vector")
            opn("vector", lambda e: e.tensor_scalar(out=VAR, in0=VAR, scalar1=1.0 / D,
                                                    scalar2=1e-5, op0=ALU.mult, op1=ALU.add))
            vv = op("vector", sDVE, lambda e: e.engine_nop())
            w("scalar", sDVE, vv[1])
            sq = op("scalar", sACT, lambda e: e.activation(out=RSTD, in_=VAR, func=AF.Sqrt))
            w("vector", sACT, sq[1])
            opn("vector", lambda e: e.reciprocal(out=RSTD, in_=RSTD))
            fence("vector")
            opn("vector", lambda e: e.tensor_scalar(out=XN, in0=XN, scalar1=RSTD,
                                                    scalar2=None, op0=ALU.mult))
            xnr = op("vector", sDVE, lambda e: e.engine_nop())
            xn_done.append(xnr)

            ncur = None
            if own:
                opn("vector", lambda e: e.reduce_sum(out=MEAN, in_=XN, axis=AXX))
                fence("vector")
                opn("vector", lambda e: e.tensor_scalar(out=MEAN, in0=MEAN, scalar1=1.0 / D,
                                                        scalar2=None, op0=ALU.mult))
                fence("vector")
                opn("vector", lambda e: e.tensor_scalar(out=QY, in0=XN, scalar1=MEAN,
                                                        scalar2=None, op0=ALU.subtract))
                fence("vector")
                opn("vector", lambda e: e.tensor_tensor(out=QB, in0=QY, in1=QY, op=ALU.mult))
                opn("vector", lambda e: e.reduce_sum(out=VAR, in_=QB, axis=AXX))
                fence("vector")
                opn("vector", lambda e: e.tensor_scalar(out=VAR, in0=VAR, scalar1=1.0 / D,
                                                        scalar2=1e-8, op0=ALU.mult, op1=ALU.add))
                vv2 = op("vector", sDVE, lambda e: e.engine_nop())
                w("scalar", sDVE, vv2[1])
                sq2 = op("scalar", sACT, lambda e: e.activation(out=RSTD, in_=VAR, func=AF.Sqrt))
                w("vector", sACT, sq2[1])
                opn("vector", lambda e: e.reciprocal(out=RSTD, in_=RSTD))
                fence("vector")
                opn("vector", lambda e: e.tensor_scalar(out=QY, in0=QY, scalar1=RSTD,
                                                        scalar2=None, op0=ALU.mult))
                fence("vector")
                opn("vector", lambda e: e.reduce_max(out=AMX, in_=QY, axis=AXX,
                                                     apply_absolute_value=True))
                fence("vector")
                opn("vector", lambda e: e.reciprocal(out=SSC, in_=AMX))
                fence("vector")
                opn("vector", lambda e: e.tensor_scalar(out=SSC, in0=SSC, scalar1=127.0,
                                                        scalar2=None, op0=ALU.mult))
                fence("vector")
                opn("vector", lambda e: e.tensor_scalar(out=QY, in0=QY, scalar1=SSC,
                                                        scalar2=None, op0=ALU.mult))
                fence("vector")
                opn("vector", lambda e, ot=ot: e.tensor_scalar(
                    out=INVC[:, ot:ot + 1], in0=AMX, scalar1=1.0 / 127.0,
                    scalar2=None, op0=ALU.mult))
                opn("vector", lambda e: e.memset(QN, -128.0))
                fence("vector")
                cur, nxt = QN, QN2
                for bit in [128.0, 64.0, 32.0, 16.0, 8.0, 4.0, 2.0, 1.0]:
                    opn("vector", lambda e, cur=cur, bit=bit: e.scalar_tensor_tensor(
                        out=QB, in0=cur, scalar=bit - 0.5, in1=QY,
                        op0=ALU.add, op1=ALU.is_le))
                    fence("vector")
                    opn("vector", lambda e, cur=cur, nxt=nxt, bit=bit: e.scalar_tensor_tensor(
                        out=nxt, in0=QB, scalar=bit, in1=cur, op0=ALU.mult, op1=ALU.add))
                    fence("vector")
                    cur, nxt = nxt, cur
                ncur = cur
                nr = op("vector", sDVE, lambda e: e.engine_nop())
                w("tensor", sDVE, nr[1])
                if t > TW // 128:
                    w("scalar", sG, nt_spills[-1][1])
                for kb in range(NKB):
                    px = PROT[kb % 2]
                    key = kb % 2
                    if key in last_evac:
                        w("tensor", last_evac[key][0], last_evac[key][1])
                    mm = op("tensor", sPE, lambda e, kb=kb, cur=cur, px=px: e.transpose(
                        out=px[:, 0:128], in_=cur[:, kb * 128:(kb + 1) * 128],
                        identity=IDENT))
                    w("scalar", sPE, mm[1])
                    ev = op("scalar", sACT, lambda e, kb=kb, px=px: e.copy(
                        out=NTB[:, kb * 128:(kb + 1) * 128], in_=px[:, 0:128]))
                    last_evac[key] = ev
                w("gpsimd", last_evac[0])
                w("gpsimd", last_evac[1])
                sp = dma("gpsimd", sG,
                         nt_dram.rearrange("p (kb tk) -> p kb tk", kb=NKB)[:, :, ot * 128:(ot + 1) * 128],
                         NTB.rearrange("p (kb c) -> p kb c", kb=NKB))
                nt_spills.append(sp)
                if dbg and t == TW // 128:
                    w("sync", sDVE, nr[1])
                    dd = dma("sync", sW, dbg_d["d_n"], cur)
                    w("vector", sW, dd[1])
            if dbg and t == TW // 128:
                w("sync", sDVE, xnr[1])
                dd = dma("sync", sW, dbg_d["d_xn"], XN)
                w("vector", sW, dd[1])
            w("tensor", sDVE, xnr[1])
            for kb in range(NKB):
                px = PROT[2 + kb % 2]
                transpose_block(XN, kb * 128, px, XNT_HI, XNT_LO, kb * TT + t * 128,
                                2 + kb % 2)
            tile_pe_end.append(sPE.n)
            tile_dve_end.append(sDVE.n)

        w("gpsimd", sDVE, sDVE.n)
        iv1 = dma("gpsimd", sG, invs_dram, INVC)
        w("gpsimd", sG, iv1[1])
        iv2 = dma("gpsimd", sG, invsr_dram.rearrange("o (tb p) -> o tb p", p=128),
                  invs_dram.rearrange("p tb -> tb p").unsqueeze(0), slow=True)
        w("gpsimd", sG, iv2[1])

        P2_DVE = sDVE.n
        P2_ACT = sACT.n
        if stop_here(2):
            w("sync", sDVE, sDVE.n); w("sync", sACT, sACT.n)
            w("sync", sPE, sPE.n); w("sync", sG, sG.n); w("sync", sW, sW.n)
            return

        wcons = {}
        pxfree = bankf

        def proj_dmajor(w_dram, tok0, ntok, src_hi, src_lo, src_stride,
                        evac_cb, post_cb, scale_gw, wbufs, whibufs, wlobufs):
            ntc = ntok // 512
            for cb in range(NKB):
                bsel = cb % 2
                if (bsel in wcons):
                    w("sync", wcons[bsel])
                ldw = dma("sync", sW, wbufs[bsel].rearrange("p (kb c) -> p kb c", kb=NKB),
                          w_dram[:, cb * 128:(cb + 1) * 128].rearrange(
                              "(kb p) c -> p kb c", p=128))
                if scale_gw:
                    w("vector", sW, ldw[1])
                    opn("vector", lambda e, bsel=bsel, cb=cb: e.tensor_scalar(
                        out=wbufs[bsel], in0=wbufs[bsel], scalar1=GWT[:, cb:cb + 1],
                        scalar2=None, op0=ALU.mult))
                    gr = op("vector", sDVE, lambda e: e.engine_nop())
                    w("scalar", sDVE, gr[1])
                else:
                    w("scalar", sW, ldw[1])
                hv = op("scalar", sACT, lambda e, bsel=bsel: e.copy(
                    out=whibufs[bsel], in_=wbufs[bsel]))
                w("vector", sACT, hv[1])
                lv = op("vector", sDVE, lambda e, bsel=bsel: e.tensor_tensor(
                    out=wlobufs[bsel], in0=wbufs[bsel], in1=whibufs[bsel],
                    op=ALU.subtract))
                w("tensor", sDVE, lv[1])
                mm = None
                for tc in range(ntc):
                    px = PROT[(cb * ntc + tc) % 4]
                    key = (cb * ntc + tc) % 4
                    if key in pxfree:
                        w("tensor", pxfree[key])
                    for kb in range(NKB):
                        wh = whibufs[bsel][:, kb * 128:(kb + 1) * 128]
                        wl = wlobufs[bsel][:, kb * 128:(kb + 1) * 128]
                        mc = kb * src_stride + tok0 + tc * 512
                        xh = src_hi[:, mc:mc + 512]
                        xl = src_lo[:, mc:mc + 512]
                        st = kb == 0
                        opn("tensor", lambda e, a=wh, b=xh, px=px, st=st: e.matmul(
                            out=px, lhsT=a, rhs=b, start=st, stop=False))
                        opn("tensor", lambda e, a=wh, b=xl, px=px: e.matmul(
                            out=px, lhsT=a, rhs=b, start=False, stop=False))
                        if kb == NKB - 1:
                            mm = op("tensor", sPE, lambda e, a=wl, b=xh, px=px: e.matmul(
                                out=px, lhsT=a, rhs=b, start=False, stop=True))
                        else:
                            opn("tensor", lambda e, a=wl, b=xh, px=px: e.matmul(
                                out=px, lhsT=a, rhs=b, start=False, stop=False))
                    pxfree[key] = evac_cb(cb, tc, px, mm)
                wcons[bsel] = (sPE, sPE.n)
                if post_cb is not None:
                    post_cb(cb)

        w("tensor", sDVE, P2_DVE)
        w("tensor", sACT, P2_ACT)
        kd_sp = []
        ktt_sp = []
        lam_sp = []
        p3_state = {"kt_dve": None, "ktt_pe": None}

        def f_evac(cb, tc, px, mm):
            if tc == 0 and p3_state["kt_dve"] is not None:
                w("scalar", sDVE, p3_state["kt_dve"][1])
            w("scalar", sPE, mm[1])
            ev = op("scalar", sACT, lambda e, tc=tc, px=px: e.activation(
                out=EH[:, tc * 512:(tc + 1) * 512], in_=px, func=AF.Sigmoid))
            return ev

        def f_post(h):
            if p3_state["ktt_pe"] is not None:
                w("vector", sPE, p3_state["ktt_pe"][1])
            if kd_sp:
                w("vector", sG, kd_sp[-1][1])
            if p3_state["kt_dve"] is not None:
                w("scalar", sDVE, p3_state["kt_dve"][1])
            g1 = op("scalar", sACT, lambda e: e.activation(out=GH, in_=EH, func=AF.Ln))
            w("vector", sACT, g1[1])
            for j in range(NCH):
                opn("vector", lambda e, j=j: e.tensor_tensor_scan(
                    out=BH[:, j * C:(j + 1) * C], data0=GH[:, j * C:(j + 1) * C],
                    data1=GH[:, j * C:(j + 1) * C], initial=0.0,
                    op0=ALU.add, op1=ALU.bypass))
            fence("vector")
            for j in range(NCH):
                opn("vector", lambda e, j=j, h=h: e.tensor_copy(
                    out=DCL[:, h * NCH + j:h * NCH + j + 1],
                    in_=BH[:, j * C + C - 1:j * C + C]))
            bd = op("vector", sDVE, lambda e: e.engine_nop())
            w("scalar", sDVE, bd[1])
            dce = op("scalar", sACT, lambda e, h=h: e.activation(
                out=DCE[:, h * NCH:(h + 1) * NCH],
                in_=DCL[:, h * NCH:(h + 1) * NCH], func=AF.Exp))
            w("vector", sACT, dce[1])
            opn("vector", lambda e, h=h: e.reciprocal(
                out=DCI[:, h * NCH:(h + 1) * NCH], in_=DCE[:, h * NCH:(h + 1) * NCH]))
            fence("vector")
            if lam_sp:
                w("scalar", sG, lam_sp[-1][1])
            lmA = op("scalar", sACT, lambda e: e.activation(
                out=LH, in_=BH[:, TW:TT], func=AF.Exp))
            e1 = op("scalar", sACT, lambda e: e.activation(
                out=GH, in_=BH, func=AF.Exp, scale=CM1[:, 0:1]))
            w("vector", sACT, e1[1])
            opn("vector", lambda e: e.tensor_scalar(out=EH, in0=EH, scalar1=-1.0,
                                                    scalar2=1.0, op0=ALU.mult, op1=ALU.add))
            fence("vector")
            for j in range(NCH):
                opn("vector", lambda e, j=j, h=h: e.scalar_tensor_tensor(
                    out=BH[:, j * C:(j + 1) * C], in0=GH[:, j * C:(j + 1) * C],
                    scalar=DCE[:, h * NCH + j:h * NCH + j + 1],
                    in1=EH[:, j * C:(j + 1) * C], op0=ALU.mult, op1=ALU.mult))
                fence("vector")
            ktr = op("vector", sDVE, lambda e: e.engine_nop())
            p3_state["kt_dve"] = ktr
            w("gpsimd", sDVE, ktr[1])
            w("gpsimd", sACT, lmA[1])
            kd_sp.append(dma("gpsimd", sG, kd_dram[h * 128:(h + 1) * 128, :], BH))
            lam_sp.append(dma("gpsimd", sG, lam_dram[h * 128:(h + 1) * 128, :], LH))
            w("tensor", sDVE, ktr[1])
            if ktt_sp:
                w("vector", sG, ktt_sp[-1][1])
            for j in range(NCH):
                px = PROT[j % 2]
                key = j % 2
                if key in last_evac:
                    w("tensor", last_evac[key][0], last_evac[key][1])
                mm = op("tensor", sPE, lambda e, j=j, px=px: e.transpose(
                    out=px[0:C, 0:128], in_=BH[:, j * C:(j + 1) * C], identity=IDENT))
                w("vector", sPE, mm[1])
                ev = op("vector", sDVE, lambda e, j=j, px=px: e.tensor_copy(
                    out=KTT[:, j * 128:(j + 1) * 128], in_=px[0:C, 0:128]))
                last_evac[key] = ev
            p3_state["ktt_pe"] = (sPE, sPE.n)
            w("gpsimd", sDVE, sDVE.n)
            ktt_sp.append(dma("gpsimd", sG,
                              ktt_dram[h].rearrange("(j p) c -> p j c", p=C),
                              KTT.rearrange("p (j c) -> p j c", j=NCH)))
            if dbg and h == 0:
                w("sync", sDVE, sDVE.n)
                dma("sync", sW, dbg_d["d_e"], GH)
                dma("sync", sW, dbg_d["d_kt"], BH)
                dd = dma("sync", sW, dbg_d["d_lam"], LH)
                w("vector", sW, dd[1])
                w("scalar", sW, dd[1])

        w("sync", sPE, sPE.n)
        w("sync", sDVE, sDVE.n)
        proj_dmajor(wf_d, 0, TT, XNT_HI, XNT_LO, TT, f_evac, f_post, False,
                    WB, WSH, WSL)

        if stop_here(3):
            w("sync", sDVE, sDVE.n); w("sync", sACT, sACT.n)
            w("sync", sPE, sPE.n); w("sync", sG, sG.n); w("sync", sW, sW.n)
            return
        p4_state = {"tr_pe": None, "vsp": None}

        def i_evac(cb, tc, px, mm):
            if tc == 0 and p4_state["tr_pe"] is not None:
                w("vector", sPE, p4_state["tr_pe"][1])
            w("vector", sPE, mm[1])
            ev = op("vector", sDVE, lambda e, tc=tc, px=px: e.tensor_copy(
                out=EH[:, tc * 512:(tc + 1) * 512], in_=px))
            return ev

        def i_post(cb):
            w("tensor", sDVE, sDVE.n)
            if p4_state["vsp"] is not None:
                w("vector", sG, p4_state["vsp"][1])
            for tb in range(NTIL):
                px = PROT[tb % 2]
                key = tb % 2
                if key in last_evac:
                    w("tensor", last_evac[key][0], last_evac[key][1])
                mm = op("tensor", sPE, lambda e, tb=tb, px=px: e.transpose(
                    out=px[:, 0:128], in_=EH[:, tb * 128:(tb + 1) * 128],
                    identity=IDENT))
                w("vector", sPE, mm[1])
                ev = op("vector", sDVE, lambda e, tb=tb, px=px: e.tensor_copy(
                    out=VSTG[:, tb * 128:(tb + 1) * 128], in_=px[:, 0:128]))
                last_evac[key] = ev
            p4_state["tr_pe"] = (sPE, sPE.n)
            w("gpsimd", sDVE, sDVE.n)
            p4_state["vsp"] = dma(
                "gpsimd", sG,
                v_dram[:, cb * 128:(cb + 1) * 128].rearrange("(tb p) c -> p tb c", p=128),
                VSTG.rearrange("p (tb c) -> p tb c", tb=NTIL))
            if dbg and cb == 0:
                w("sync", sDVE, sDVE.n)
                dd = dma("sync", sW, dbg_d["d_v"], VSTG)
                w("vector", sW, dd[1])

        proj_dmajor(wi_d, 0, TT, XNT_HI, XNT_LO, TT, i_evac, i_post, False,
                    WB, WSH, WSL)

        if stop_here(4):
            w("sync", sDVE, sDVE.n); w("sync", sACT, sACT.n)
            w("sync", sPE, sPE.n); w("sync", sG, sG.n); w("sync", sW, sW.n)
            return
        p5_state = {"qdve": None}

        def q_evac(cb, tc, px, mm):
            if tc == 0 and p5_state["qdve"] is not None:
                w("scalar", sDVE, p5_state["qdve"][1])
            w("scalar", sPE, mm[1])
            ev = op("scalar", sACT, lambda e, tc=tc, px=px: e.activation(
                out=QSH[:, tc * 512:(tc + 1) * 512], in_=px, func=AF.Silu))
            return ev

        def q_post(h):
            w("gpsimd", sG, lam_sp[h][1])
            lr = dma("gpsimd", sG, LH, lam_dram[h * 128:(h + 1) * 128, :])
            w("vector", sG, lr[1])
            w("vector", sACT, sACT.n)
            opn("vector", lambda e: e.scalar_tensor_tensor(
                out=QSH, in0=QSH, scalar=DKS, in1=LH, op0=ALU.mult, op1=ALU.mult))
            qr = op("vector", sDVE, lambda e: e.engine_nop())
            p5_state["qdve"] = qr
            w("gpsimd", sDVE, qr[1])
            dma("gpsimd", sG, qt_dram[h * 128:(h + 1) * 128, :], QSH)
            if dbg and h == 0:
                w("sync", sDVE, qr[1])
                dd = dma("sync", sW, dbg_d["d_qt"], QSH)
                w("scalar", sW, dd[1])

        proj_dmajor(wq_d, TW, TS, XNT_HI, XNT_LO, TT, q_evac, q_post, False,
                    WB, WSH, WSL)

        P5_G = sG.n
        P5_DVE = sDVE.n
        P5_PE = sPE.n

        if stop_here(5):
            w("sync", sDVE, sDVE.n); w("sync", sACT, sACT.n)
            w("sync", sPE, sPE.n); w("sync", sG, sG.n); w("sync", sW, sW.n)
            return
        w("sync", sPE, P5_PE)
        w("sync", sDVE, P5_DVE)
        w("sync", sACT, sACT.n)
        w("gpsimd", sG, P5_G)
        w("gpsimd", sDVE, P5_DVE)
        w("gpsimd", sPE, P5_PE)
        nrl = dma("gpsimd", sG, NT7, nt_dram)
        iv3 = dma("gpsimd", sG, INVSB, invsr_dram[0:1, :].partition_broadcast(128))
        w("vector", sDVE, P5_DVE)
        w("vector", sG, iv3[1])
        w("vector", sPE, P5_PE)
        opn("vector", lambda e: e.memset(FFA, 0.0))
        ffz = op("vector", sDVE, lambda e: e.engine_nop())

        NHG = 16
        HBG = 4
        fst_cons = {}
        sgd_cons = {}
        y_free = None
        big_free = None
        dn_pe = None
        sg_guard = None
        for hg in range(NHG):
            if sg_guard is not None:
                w("scalar", sPE, sg_guard)
            for wsel, (wsrc, sgdst) in enumerate([(wgt_d, SGW), (wut_d, SGU)]):
                for qr in range(4):
                    if 0 in fst_cons:
                        w("sync", fst_cons[0])
                    ld = dma("sync", sW,
                             FST[0].rearrange("p (kb c) -> p kb c", kb=4),
                             wsrc[qr * 512:(qr + 1) * 512,
                                  hg * 512:(hg + 1) * 512].rearrange(
                                 "(kb p) c -> p kb c", p=128))
                    w("scalar", sW, ld[1])
                    sgn = op("scalar", sACT, lambda e, sgdst=sgdst,
                             qr=qr: e.sign(
                        out=sgdst[:, qr * 2048:(qr + 1) * 2048], in_=FST[0]))
                    fst_cons[0] = sgn
            sgw_rdy = sACT.n
            w("tensor", sG, nrl[1])
            w("tensor", sACT, sgw_rdy)
            ym = None
            for hb in range(HBG):
                if y_free is not None:
                    w("tensor", y_free)
                for kb in range(NKB):
                    lg = SGW[:, kb * 512 + hb * 128: kb * 512 + (hb + 1) * 128]
                    lu = SGU[:, kb * 512 + hb * 128: kb * 512 + (hb + 1) * 128]
                    nt0 = NT7[:, kb * TS:kb * TS + 512]
                    nt1 = NT7[:, kb * TS + 512:kb * TS + 1024]
                    st = kb == 0
                    sp = kb == NKB - 1
                    opn("tensor", lambda e, lg=lg, nt0=nt0, st=st, sp=sp: e.matmul(
                        out=pA, lhsT=lg, rhs=nt0, start=st, stop=sp))
                    opn("tensor", lambda e, lg=lg, nt1=nt1, st=st, sp=sp: e.matmul(
                        out=pB, lhsT=lg, rhs=nt1, start=st, stop=sp))
                    opn("tensor", lambda e, lu=lu, nt0=nt0, st=st, sp=sp: e.matmul(
                        out=pC, lhsT=lu, rhs=nt0, start=st, stop=sp))
                    if sp:
                        ym = op("tensor", sPE, lambda e, lu=lu, nt1=nt1, st=st: e.matmul(
                            out=pD, lhsT=lu, rhs=nt1, start=st, stop=True))
                    else:
                        opn("tensor", lambda e, lu=lu, nt1=nt1, st=st: e.matmul(
                            out=pD, lhsT=lu, rhs=nt1, start=st, stop=False))
                w("vector", sPE, ym[1])
                opn("vector", lambda e: e.tensor_tensor(out=GT7[:, 0:512], in0=pA,
                                                        in1=INVSB[:, 0:512], op=ALU.mult))
                opn("vector", lambda e: e.tensor_tensor(out=GT7[:, 512:1024], in0=pB,
                                                        in1=INVSB[:, 512:1024], op=ALU.mult))
                opn("vector", lambda e: e.tensor_tensor(out=UT7[:, 0:512], in0=pC,
                                                        in1=INVSB[:, 0:512], op=ALU.mult))
                opn("vector", lambda e: e.tensor_tensor(out=UT7[:, 512:1024], in0=pD,
                                                        in1=INVSB[:, 512:1024], op=ALU.mult))
                gu = op("vector", sDVE, lambda e: e.engine_nop())
                y_free = gu
                w("scalar", sDVE, gu[1])
                sg = op("scalar", sACT, lambda e: e.activation(out=GT7, in_=GT7,
                                                               func=AF.Silu))
                w("vector", sACT, sg[1])
                if dn_pe is not None and hb == 0:
                    w("vector", dn_pe)
                opn("vector", lambda e: e.tensor_tensor(out=GT7, in0=GT7, in1=UT7,
                                                        op=ALU.mult))
                opn("vector", lambda e, hb=hb: e.tensor_copy(
                    out=PH7[:, hb * TS:(hb + 1) * TS], in_=GT7))
                fence("vector")
                op("vector", sDVE, lambda e, hb=hb: e.tensor_tensor(
                    out=PL7[:, hb * TS:(hb + 1) * TS], in0=GT7,
                    in1=PH7[:, hb * TS:(hb + 1) * TS], op=ALU.subtract))
            p_rdy = sDVE.n
            sg_guard_y = sPE.n

            for hb in range(HBG):
                if sgd_cons.get(hb) is not None:
                    w("sync", sgd_cons[hb])
                if 0 in fst_cons:
                    w("sync", fst_cons[0])
                ld = dma("sync", sW, FST[0],
                         wdt_d[hg * 512 + hb * 128: hg * 512 + (hb + 1) * 128, :])
                w("scalar", sW, ld[1])
                fst_cons[0] = op("scalar", sACT, lambda e, hb=hb: e.sign(
                    out=SGD[hb], in_=FST[0]))
            sgd_rdy = sACT.n
            w("tensor", sDVE, p_rdy)
            w("tensor", sACT, sgd_rdy)
            nbf = []
            for tb in range(OTIL):
                if nbf:
                    w("tensor", nbf[-1])
                elif big_free is not None:
                    w("tensor", big_free[-1])
                mm = None
                for hb in range(HBG):
                    for ps, pt in [(PH7, 0), (PL7, 1)]:
                        lt = ps[:, hb * TS + tb * 128: hb * TS + (tb + 1) * 128]
                        st = (hb == 0 and pt == 0)
                        lastg = (hb == HBG - 1 and pt == 1)
                        for ncol in range(4):
                            if lastg and ncol == 3:
                                mm = op("tensor", sPE, lambda e, lt=lt, hb=hb,
                                        ncol=ncol, st=st: e.matmul(
                                    out=pBIG[:, ncol * 512:(ncol + 1) * 512],
                                    lhsT=lt, rhs=SGD[hb][:, ncol * 512:(ncol + 1) * 512],
                                    start=st, stop=True))
                            else:
                                opn("tensor", lambda e, lt=lt, hb=hb, ncol=ncol,
                                    st=st, lastg=lastg: e.matmul(
                                    out=pBIG[:, ncol * 512:(ncol + 1) * 512],
                                    lhsT=lt, rhs=SGD[hb][:, ncol * 512:(ncol + 1) * 512],
                                    start=st, stop=lastg))
                w("vector", sPE, mm[1])
                if hg == 0 and tb == 0:
                    w("vector", sDVE, ffz[1])
                ev = op("vector", sDVE, lambda e, tb=tb: e.tensor_tensor(
                    out=FFA[:, tb * D:(tb + 1) * D], in0=FFA[:, tb * D:(tb + 1) * D],
                    in1=pBIG, op=ALU.add))
                nbf.append(ev)
            for hb in range(HBG):
                sgd_cons[hb] = (sPE, sPE.n)
            big_free = nbf
            dn_pe = (sPE, sPE.n)
            sg_guard = sg_guard_y

        w("gpsimd", sDVE, sDVE.n)
        ffsp = dma("gpsimd", sG, ffn_dram.rearrange("(tb p) c -> p tb c", p=128),
                   FFA.rearrange("p (tb c) -> p tb c", tb=OTIL))
        if dbg:
            w("sync", sDVE, sDVE.n)
            dma("sync", sW, dbg_d["d_ffn"], FFA[:, 0:D])

        P7_G = sG.n
        P7_DVE = sDVE.n
        P7_PE = sPE.n

        if stop_here(7):
            w("sync", sDVE, sDVE.n); w("sync", sACT, sACT.n)
            w("sync", sPE, sPE.n); w("sync", sG, sG.n); w("sync", sW, sW.n)
            return
        def head_load(h, s):
            hs = HS[s]
            dma("gpsimd", sG, hs["kd"], kd_dram[h * 128:(h + 1) * 128, :])
            dma("gpsimd", sG, hs["kt"].rearrange("p (j c) -> p j c", j=NCH),
                ktt_dram[h].rearrange("(j p) c -> p j c", p=C))
            dma("gpsimd", sG, hs["vh"].rearrange("p (j c) -> p j c", j=NCH),
                v_dram[:, h * 128:(h + 1) * 128].rearrange("(j p) c -> p j c", p=C))
            return dma("gpsimd", sG, hs["qh"], qt_dram[h * 128:(h + 1) * 128, :])

        w("gpsimd", sG, P7_G)
        w("gpsimd", sDVE, P7_DVE)
        w("gpsimd", sPE, P7_PE)
        hload = [head_load(0, 0), head_load(1, 1)]
        w("vector", sG, ffsp[1])

        obank_free = [None, None]
        dh_free = [None, None]
        st_free = [None, None]
        qh_free = [None, None]
        hx_done = [None, None]
        for h in range(NH):
            s = h % 2
            hs = HS[s]
            HX = HH[s]
            w("vector", sG, hload[h][1])
            if dbg and h == 14:
                w("sync", sG, hload[h][1])
                dma("sync", sW, dbg_d["d_skd"], hs["kd"])
                dma("sync", sW, dbg_d["d_skt"], hs["kt"])
                dma("sync", sW, dbg_d["d_svh"], hs["vh"])
                dd = dma("sync", sW, dbg_d["d_sqh"], hs["qh"])
                w("vector", sW, dd[1])
                w("gpsimd", sW, dd[1])
            opn("vector", lambda e, HX=HX: e.memset(HX, 0.0))
            hz = op("vector", sDVE, lambda e: e.engine_nop())
            w("tensor", sDVE, hz[1])
            pd = pBIG[:, (2 + s) * 512:(2 + s) * 512 + 128]
            for j in range(WCH):
                if dh_free[s] is not None:
                    w("tensor", dh_free[s])
                mm = op("tensor", sPE, lambda e, hs=hs, j=j, pd=pd: e.matmul(
                    out=pd, lhsT=hs["kt"][:, j * 128:(j + 1) * 128],
                    rhs=hs["vh"][:, j * 128:(j + 1) * 128], start=True, stop=True))
                w("vector", sPE, mm[1])
                opn("vector", lambda e, HX=HX, h=h, j=j, pd=pd: e.scalar_tensor_tensor(
                    out=HX, in0=HX, scalar=DCE[:, h * NCH + j:h * NCH + j + 1],
                    in1=pd, op0=ALU.mult, op1=ALU.add))
                dh_free[s] = op("vector", sDVE, lambda e: e.engine_nop())
            opn("vector", lambda e, HX=HX: e.tensor_scalar(
                out=HX, in0=HX, scalar1=WMK[:, 0:1], scalar2=None, op0=ALU.mult))
            hm = op("vector", sDVE, lambda e: e.engine_nop())
            w("tensor", sDVE, hm[1])
            if dbg and h in (0, 14):
                w("sync", sDVE, hm[1])
                dd = dma("sync", sW, dbg_d["d_H"] if h == 0 else
                         dbg_d["d_Hs"].rearrange("p (i c) -> p i c", i=6)[:, 0:1, :], HX)
                w("tensor", sW, dd[1])
            for j in range(WCH, NCH):
                jj = j - WCH
                pS = pA if s == 0 else pB
                pO = pC if s == 0 else pD
                sm = SMB[s]
                qhat = QHB[s]
                if dbg and h == 14 and 0 <= jj <= 4:
                    w("sync", sDVE, sDVE.n)
                    ddh = dma("sync", sW,
                              dbg_d["d_Hs"].rearrange("p (i c) -> p i c", i=6)[:, jj + 1:jj + 2, :],
                              HX)
                    w("vector", sW, ddh[1])
                    w("tensor", sW, ddh[1])
                if qh_free[s] is not None:
                    w("vector", qh_free[s])
                opn("vector", lambda e, hs=hs, qhat=qhat, jj=jj, h=h, j=j: e.tensor_scalar(
                    out=qhat, in0=hs["qh"][:, jj * C:(jj + 1) * C],
                    scalar1=DCI[:, h * NCH + j:h * NCH + j + 1],
                    scalar2=None, op0=ALU.mult))
                qhr = op("vector", sDVE, lambda e: e.engine_nop())
                if st_free[s] is not None:
                    w("tensor", st_free[s])
                w("tensor", sDVE, qhr[1])
                m1 = op("tensor", sPE, lambda e, hs=hs, j=j, qhat=qhat, pS=pS: e.matmul(
                    out=pS[0:C, 0:C], lhsT=hs["kd"][:, j * C:(j + 1) * C],
                    rhs=qhat, start=True, stop=True))
                qh_free[s] = (sPE, m1[1])
                w("vector", sPE, m1[1])
                opn("vector", lambda e, sm=sm, pS=pS: e.tensor_tensor(
                    out=sm, in0=pS[0:C, 0:C], in1=TRI, op=ALU.mult))
                smr = op("vector", sDVE, lambda e: e.engine_nop())
                st_free[s] = smr
                w("tensor", sDVE, smr[1])
                if obank_free[s] is not None:
                    w("tensor", obank_free[s])
                ro = (jj * C) % 128
                tp = None if ro == 0 else (0, 64)
                opn("tensor", lambda e, sm=sm, hs=hs, j=j, pO=pO, ro=ro, tp=tp: e.matmul(
                    out=pO[ro:ro + C, 0:128], lhsT=sm,
                    rhs=hs["vh"][:, j * 128:(j + 1) * 128],
                    start=True, stop=False, tile_position=tp))
                m2 = op("tensor", sPE, lambda e, hs=hs, jj=jj, HX=HX, pO=pO, ro=ro,
                        tp=tp: e.matmul(
                    out=pO[ro:ro + C, 0:128], lhsT=hs["qh"][:, jj * C:(jj + 1) * C],
                    rhs=HX, start=False, stop=True, tile_position=tp))
                if dh_free[s] is not None:
                    w("tensor", dh_free[s])
                m3 = op("tensor", sPE, lambda e, hs=hs, j=j, pd=pd: e.matmul(
                    out=pd, lhsT=hs["kt"][:, j * 128:(j + 1) * 128],
                    rhs=hs["vh"][:, j * 128:(j + 1) * 128], start=True, stop=True))
                w("vector", sPE, m2[1])
                tb = (jj * C) // 128
                if dbg and h == 14 and jj == 4:
                    opn("vector", lambda e, pO=pO: e.tensor_copy(out=OSQ[:, 0:128],
                                                                 in_=pO[:, 0:128]))
                    dcp = op("vector", sDVE, lambda e: e.engine_nop())
                    w("sync", sDVE, dcp[1])
                    dma("sync", sW, dbg_d["d_sm"], sm)
                    dma("sync", sW, dbg_d["d_qh2"], qhat)
                    dma("sync", sW, dbg_d["d_H2"], HX)
                    dd9 = dma("sync", sW, dbg_d["d_po"], OSQ[:, 0:128])
                    w("vector", sW, dd9[1])
                    w("tensor", sW, dd9[1])
                obank_free[s] = op("vector", sDVE, lambda e, tb=tb, ro=ro, h=h, pO=pO:
                                   e.tensor_copy(
                    out=OBUF[ro:ro + C, tb * D + h * 128: tb * D + (h + 1) * 128],
                    in_=pO[ro:ro + C, 0:128]))
                w("vector", sPE, m3[1])
                if dbg and h == 14 and jj == 0:
                    w("sync", sPE, m3[1])
                    dma("sync", sW, dbg_d["d_kt8"], hs["kt"][:, 8 * 128:9 * 128])
                    dma("sync", sW, dbg_d["d_vh8"], hs["vh"][:, 8 * 128:9 * 128])
                    opn("vector", lambda e, pd=pd: e.tensor_copy(out=OSQ[:, 0:128], in_=pd))
                    dcq = op("vector", sDVE, lambda e: e.engine_nop())
                    w("sync", sDVE, dcq[1])
                    dma("sync", sW, dbg_d["d_pd"], OSQ[:, 0:128])
                    ddd = dma("sync", sW, dbg_d["d_dce"], DCE[:, 14 * NCH:15 * NCH])
                    w("vector", sW, ddd[1])
                opn("vector", lambda e, HX=HX, h=h, j=j, pd=pd: e.scalar_tensor_tensor(
                    out=HX, in0=HX, scalar=DCE[:, h * NCH + j:h * NCH + j + 1],
                    in1=pd, op0=ALU.mult, op1=ALU.add))
                dh_free[s] = op("vector", sDVE, lambda e: e.engine_nop())
            if h + 2 < NH:
                w("gpsimd", sDVE, sDVE.n)
                hload.append(head_load(h + 2, s))
        P8_DVE = sDVE.n
        for k in range(4):
            bankf[k] = (sDVE, P8_DVE)
        if dbg:
            w("sync", sDVE, P8_DVE)
            dd = dma("sync", sW, dbg_d["d_O"], OBUF)
            w("vector", sW, dd[1])

        if stop_here(8):
            w("sync", sDVE, sDVE.n); w("sync", sACT, sACT.n)
            w("sync", sPE, sPE.n); w("sync", sG, sG.n); w("sync", sW, sW.n)
            return
        RS2 = STA[:, 24:25]
        w("vector", sDVE, P8_DVE)
        for tb in range(OTIL):
            ob = OBUF[:, tb * D:(tb + 1) * D]
            opn("vector", lambda e, ob=ob: e.tensor_tensor(out=OSQ, in0=ob, in1=ob,
                                                           op=ALU.mult))
            opn("vector", lambda e: e.reduce_sum(out=RMS, in_=OSQ, axis=AXX))
            fence("vector")
            opn("vector", lambda e: e.tensor_scalar(out=RMS, in0=RMS, scalar1=1.0 / D,
                                                    scalar2=1e-5, op0=ALU.mult, op1=ALU.add))
            rr = op("vector", sDVE, lambda e: e.engine_nop())
            w("scalar", sDVE, rr[1])
            sq = op("scalar", sACT, lambda e: e.activation(out=RS2, in_=RMS, func=AF.Sqrt))
            w("vector", sACT, sq[1])
            opn("vector", lambda e: e.reciprocal(out=RS2, in_=RS2))
            fence("vector")
            opn("vector", lambda e, ob=ob: e.tensor_scalar(out=ob, in0=ob, scalar1=RS2,
                                                           scalar2=None, op0=ALU.mult))
            onr = op("vector", sDVE, lambda e: e.engine_nop())
            w("tensor", sDVE, onr[1])
            for kb in range(NKB):
                px = PROT[kb % 2]
                transpose_block(ob, kb * 128, px, OTH, OTL, kb * TS + tb * 128, kb % 2)
        P9_DVE = sDVE.n
        if dbg:
            w("sync", sDVE, P9_DVE)
            dma("sync", sW, dbg_d["d_oT"], OBUF)
            for kb in range(NKB):
                opn("vector", lambda e, kb=kb: e.tensor_tensor(
                    out=OSQ[:, 0:1024], in0=OTH[:, kb * TS:(kb + 1) * TS],
                    in1=OTL[:, kb * TS:(kb + 1) * TS], op=ALU.add))
                dv = op("vector", sDVE, lambda e: e.engine_nop())
                w("sync", sDVE, dv[1])
                dd = dma("sync", sW, dbg_d["d_ot2"].rearrange(
                    "p (kb t) -> p kb t", kb=NKB)[:, kb:kb + 1, :], OSQ[:, 0:1024])
                w("vector", sW, dd[1])

        if stop_here(9):
            w("sync", sDVE, sDVE.n); w("sync", sACT, sACT.n)
            w("sync", sPE, sPE.n); w("sync", sG, sG.n); w("sync", sW, sW.n)
            return
        w("tensor", sDVE, P9_DVE)
        w("tensor", sACT, sACT.n)

        def o_evac(cb, tc, px, mm):
            w("vector", sPE, mm[1])
            ev = op("vector", sDVE, lambda e, cb=cb, tc=tc, px=px: e.tensor_copy(
                out=ATT[:, cb * TS + tc * 512: cb * TS + (tc + 1) * 512], in_=px))
            return ev

        wcons.clear()
        proj_dmajor(wo_d, 0, TS, OTH, OTL, TS, o_evac, None, True,
                    WOB, WOH, WOL)
        P10_DVE = sDVE.n
        if dbg:
            w("sync", sDVE, P10_DVE)
            dma("sync", sW, dbg_d["d_at"], ATT[:, 0:TS])

        if stop_here(10):
            w("sync", sDVE, sDVE.n); w("sync", sACT, sACT.n)
            w("sync", sPE, sPE.n); w("sync", sG, sG.n); w("sync", sW, sW.n)
            return
        w("sync", sDVE, P10_DVE)
        st_sem = []
        for tb in range(OTIL):
            if tb >= 1:
                w("sync", sDVE, sDVE.n)
            xl = dma("sync", sW, XT11, x_d[TW + tb * 128: TW + (tb + 1) * 128, :])
            if tb == 0:
                w("gpsimd", sDVE, P10_DVE)
            else:
                w("gpsimd", sDVE, sDVE.n)
            w("gpsimd", sG, ffsp[1])
            fl = dma("gpsimd", sG, FT11, ffn_dram[tb * 128:(tb + 1) * 128, :])
            w("vector", sW, xl[1])
            w("vector", sG, fl[1])
            if st_sem:
                w("vector", sW, st_sem[-1][1])
            if tb == 0:
                w("tensor", sDVE, P10_DVE)
            opn("vector", lambda e: e.tensor_tensor(
                out=OUT11, in0=XT11, in1=FT11, op=ALU.add))
            fence("vector")
            for kb in range(NKB):
                px = PROT[kb % 2]
                key = kb % 2
                if key in last_evac:
                    w("tensor", last_evac[key][0], last_evac[key][1])
                mm = op("tensor", sPE, lambda e, kb=kb, tb=tb, px=px: e.transpose(
                    out=px[:, 0:128],
                    in_=ATT[:, kb * TS + tb * 128: kb * TS + (tb + 1) * 128],
                    identity=IDENT))
                w("vector", sPE, mm[1])
                ev = op("vector", sDVE, lambda e, kb=kb, px=px: e.tensor_tensor(
                    out=OUT11[:, kb * 128:(kb + 1) * 128], in0=px[:, 0:128],
                    in1=OUT11[:, kb * 128:(kb + 1) * 128], op=ALU.add))
                last_evac[key] = ev
            w("sync", sDVE, sDVE.n)
            st_sem.append(dma("sync", sW, out_d[tb * 128:(tb + 1) * 128, :], OUT11))
        w("sync", sW, sW.n)
        w("gpsimd", sG, sG.n)

    with (
        nc.semaphore("sw") as _sw,
        nc.semaphore("sg") as _sg,
        nc.semaphore("spe") as _spe,
        nc.semaphore("sdve") as _sdve,
        nc.semaphore("sact") as _sact,
        nc.Block() as block,
    ):
        S["w"] = Sem(_sw)
        S["g"] = Sem(_sg)
        S["pe"] = Sem(_spe)
        S["dve"] = Sem(_sdve)
        S["act"] = Sem(_sact)

        emit_all()

        @block.sync
        def _(e):
            for fn in plan.items["sync"]:
                fn(e)

        @block.gpsimd
        def _(e):
            for fn in plan.items["gpsimd"]:
                fn(e)

        @block.tensor
        def _(e):
            for fn in plan.items["tensor"]:
                fn(e)

        @block.vector
        def _(e):
            for fn in plan.items["vector"]:
                fn(e)

        @block.scalar
        def _(e):
            for fn in plan.items["scalar"]:
                fn(e)

    return nc


def kernel(L0_attn_x, L0_attn_Wq, L0_attn_Wf, L0_attn_Wi, L0_attn_Wo,
           L0_attn_g_weight, L0_ffn_W_g, L0_ffn_W_u, L0_ffn_W_d):
    x = np.ascontiguousarray(np.asarray(L0_attn_x, dtype=np.float32))
    wq = np.ascontiguousarray(np.asarray(L0_attn_Wq, dtype=np.float32))
    wf = np.ascontiguousarray(np.asarray(L0_attn_Wf, dtype=np.float32))
    wi = np.ascontiguousarray(np.asarray(L0_attn_Wi, dtype=np.float32))
    wo = np.ascontiguousarray(np.asarray(L0_attn_Wo, dtype=np.float32))
    gw = np.ascontiguousarray(np.asarray(L0_attn_g_weight, dtype=np.float32))
    wgt = np.ascontiguousarray(np.asarray(L0_ffn_W_g, dtype=np.float32).T)
    wut = np.ascontiguousarray(np.asarray(L0_ffn_W_u, dtype=np.float32).T)
    wdt = np.ascontiguousarray(np.asarray(L0_ffn_W_d, dtype=np.float32).T)

    nc = build()
    ones = np.ones((128, 1), dtype=np.float32)
    zeros = np.zeros((128, 1), dtype=np.float32)
    in_maps = []
    for core in range(8):
        b, s = core // 4, core % 4
        st = s * TS
        if s == 0:
            x_ext = np.concatenate([x[b, 0:TW], x[b, 0:TS]], axis=0)
            wm = zeros
        else:
            x_ext = x[b, st - TW: st + TS]
            wm = ones
        in_maps.append({
            "x": np.ascontiguousarray(x_ext), "wq": wq, "wf": wf, "wi": wi,
            "wo": wo, "gw": gw, "wgt": wgt, "wut": wut, "wdt": wdt, "wmask": wm,
        })
    res = run_bass_kernel_spmd(nc, in_maps, core_ids=list(range(8)))
    out = np.empty((2, 4096, D), dtype=np.float32)
    for core in range(8):
        b, s = core // 4, core % 4
        out[b, s * TS:(s + 1) * TS] = res.results[core]["out"]
    return out
